# version 01 (final)
# baseline (speedup 1.0000x reference)
import contextlib
import numpy as np
import ml_dtypes
import concourse.bass as bass
import concourse.mybir as mybir
from concourse.bass_utils import run_bass_kernel_spmd

F32 = mybir.dt.float32
BF16 = mybir.dt.bfloat16
AF = mybir.ActivationFunctionType
ALU = mybir.AluOpType
AX = mybir.AxisListType
NPBF = ml_dtypes.bfloat16

BATCH, T, D = 2, 8192, 1024
H, PH, NS = 32, 64, 64
DI = 2 * D
DIP = 2 * DI + 2 * NS + H
TOK = 2048
NT = TOK // 128
QC = 64
LOG99 = float(np.log(np.float32(0.99)))
NEG = -30000.0
EPS = 1e-5


class Sem:
    def __init__(self, nc, name):
        self.h = nc.alloc_semaphore(name)
        self.n = 0

    def inc(self, inst, k=1):
        inst.then_inc(self.h, k)
        self.n += k
        return self.n

    def dma(self, inst):
        return self.inc(inst, 16)


class Prog:
    ENGINES = ("tensor", "vector", "scalar", "gpsimd", "sync")

    def __init__(self, name):
        self.nc = bass.Bass("TRN2", target_bir_lowering=False, name=name,
                            detect_race_conditions=False)
        self.sem = {}
        self.phase = 0
        self.bar = {e: Sem(self.nc, f"bar_{e}") for e in self.ENGINES}

    def S(self, name):
        if name not in self.sem:
            self.sem[name] = Sem(self.nc, name)
        return self.sem[name]

    def barrier(self):
        self.phase += 1
        ph = self.phase
        bar = self.bar
        dma_sems = [v for k, v in self.sem.items()
                    if k in ("ld", "gld", "sld", "so", "gso", "scan_ld")]
        with self.nc.Block() as blk:
            def mk(engname):
                def _(e):
                    if engname == "sync":
                        for ds in dma_sems:
                            e.wait_ge(ds.h, ds.n)
                    e.nop().then_inc(bar[engname].h, 1)
                    for en in self.ENGINES:
                        e.wait_ge(bar[en].h, ph)
                return _
            blk.tensor(mk("tensor"))
            blk.vector(mk("vector"))
            blk.scalar(mk("scalar"))
            blk.gpsimd(mk("gpsimd"))
            blk.sync(mk("sync"))


def host_constants():
    c = {}
    c["eye128"] = np.eye(128, dtype=np.float32)
    i, j = np.indices((64, 64))
    ut = (i <= j).astype(np.float32)
    sl = (i > j).astype(np.float32)
    z = np.zeros_like(ut)
    one = np.ones((64, 64), np.float32)
    c["bdut"] = np.block([[ut, z], [z, ut]])
    c["bdsl"] = np.block([[sl, z], [z, sl]])
    c["bdones"] = np.block([[one, z], [z, one]])
    c["ut2"] = np.vstack([ut, ut])
    cn = np.where(j < i, NEG, 0.0).astype(np.float32)
    c["cn2"] = np.vstack([cn, cn])
    c["ones128sq"] = np.ones((128, 128), np.float32)
    c["zero128"] = np.zeros((128, 128), np.float32)
    oh = np.zeros((128, 128), np.float32); oh[0:64, 64:128] = 1.0
    c["oneshalf"] = oh
    c["onesA64"] = np.concatenate([np.ones((64, 64), np.float32), np.zeros((64, 64), np.float32)])
    c["onesB64"] = np.concatenate([np.zeros((64, 64), np.float32), np.ones((64, 64), np.float32)])
    c["onescol128"] = np.ones((128, 64), np.float32)
    c["eye64"] = np.eye(64, dtype=np.float32)
    c["eyeB64"] = np.concatenate([np.zeros((64, 64), np.float32), np.eye(64, dtype=np.float32)])
    tri = np.where(np.arange(128)[:, None] > np.arange(128)[None, :], NEG, 0.0)
    c["triT"] = tri.astype(np.float32)
    return c


CONST_L1 = ("eye128", "bdut", "bdsl", "bdones", "ut2", "cn2", "ones128sq",
            "zero128", "oneshalf", "onesA64", "onesB64", "onescol128", "eye64", "eyeB64")


SCAN_STAGE = 99
DEBUG_OUT = False


def build_L1(stop_after=99, scan_pairs=16):
    P = Prog("mk_l1")
    nc = P.nc
    C = host_constants()

    x_in = nc.dram_tensor("x", [TOK, D], F32, kind="ExternalInput").ap()
    wi_in = nc.dram_tensor("wi", [DIP, D], F32, kind="ExternalInput").ap()
    wq_in = nc.dram_tensor("wq", [16, D, 64], F32, kind="ExternalInput").ap()
    wk_in = nc.dram_tensor("wk", [16, D, 64], F32, kind="ExternalInput").ap()
    wv_in = nc.dram_tensor("wv", [16, D, 64], F32, kind="ExternalInput").ap()
    sim_in = nc.dram_tensor("sim", [D, 16], F32, kind="ExternalInput").ap()
    gates_in = nc.dram_tensor("gates", [1, 16], F32, kind="ExternalInput").ap()
    alog_in = nc.dram_tensor("alog", [1, H], F32, kind="ExternalInput").ap()
    dtb_in = nc.dram_tensor("dtb", [1, H], F32, kind="ExternalInput").ap()
    dpf_in = nc.dram_tensor("dpf", [1, DI], F32, kind="ExternalInput").ap()
    niw_in = nc.dram_tensor("niw", [1, D], F32, kind="ExternalInput").ap()
    nib_in = nc.dram_tensor("nib", [1, D], F32, kind="ExternalInput").ap()
    cin = {n: nc.dram_tensor("c_" + n, list(C[n].shape), F32, kind="ExternalInput").ap()
           for n in CONST_L1}

    o_q = nc.dram_tensor("o_q", [TOK, 64], F32, kind="ExternalOutput").ap()
    o_k = nc.dram_tensor("o_k", [TOK, 64], F32, kind="ExternalOutput").ap()
    o_v = nc.dram_tensor("o_v", [TOK, 64], F32, kind="ExternalOutput").ap()
    o_rw = nc.dram_tensor("o_rw", [TOK, 16], F32, kind="ExternalOutput").ap()
    o_z = nc.dram_tensor("o_z", [TOK, DI], F32, kind="ExternalOutput").ap()
    o_yp = nc.dram_tensor("o_yp", [TOK, DI], F32, kind="ExternalOutput").ap()
    o_cs = nc.dram_tensor("o_cs", [TOK, H], F32, kind="ExternalOutput").ap()
    o_cfm = nc.dram_tensor("o_cfm", [64, TOK], F32, kind="ExternalOutput").ap()
    o_hend = nc.dram_tensor("o_hend", [128, DI], F32, kind="ExternalOutput").ap()
    o_sdec = nc.dram_tensor("o_sdec", [1, 2 * H], F32, kind="ExternalOutput").ap()

    row_dr = nc.dram_tensor("row_dr", [1, 64], F32, kind="Internal").ap()
    xn_dr = nc.dram_tensor("xn_dr", [TOK, D], BF16, kind="Internal").ap()
    xa_dr = nc.dram_tensor("xa_dr", [TOK, D], BF16, kind="Internal").ap()
    wibf_dr = nc.dram_tensor("wibf_dr", [DIP, D], BF16, kind="Internal").ap()
    xs_dr = nc.dram_tensor("xs_dr", [TOK, DI], BF16, kind="Internal").ap()

    sb = nc.alloc_sbuf_tensor
    stk = contextlib.ExitStack()
    def scoped(name, shape, dtype):
        return stk.enter_context(nc.sbuf_tensor(name, shape, dtype))

    K = {n: sb("k_" + n, list(C[n].shape), BF16) for n in CONST_L1}
    ut2_f = sb("ut2_f", [128, QC], F32)
    ones128_f = sb("ones128_f", [128, 64], F32)
    ST = {n: sb("st_" + n, [128, NT], F32) for n in
          ("s1", "s2", "mu", "var", "rs", "p", "al", "rsp", "rn", "bxn", "bxa", "tmp")}
    xn_fm = sb("xn_fm", [128, 8, TOK], BF16)
    bt_bf = sb("bt_bf", [128, NT, 64], BF16)
    dt_f = sb("dt_f", [128, NT, H], F32)
    la_hi = sb("la_hi", [128, NT, H], BF16)
    dtb_bc = sb("dtb_bc", [128, H], F32)
    a_bc = sb("a_bc", [128, H], F32)
    alog_row = sb("alog_row", [1, H], F32)
    bfm_bf = sb("bfm_bf", [64, TOK], BF16)
    cfm_bf = sb("cfm_bf", [64, TOK], BF16)
    sim_f = sb("sim_f", [128, 8, 16], F32)
    simsq = sb("simsq", [128, 8, 16], F32)
    sim_nbf = sb("sim_nbf", [128, 8, 16], BF16)
    sg_row = sb("sg_row", [1, 16], F32)
    sg_e = sb("sg_e", [1, 16], F32)
    sg_r = sb("sg_r", [1, 16], F32)
    sg_bc = sb("sg_bc", [128, 16], F32)
    wit = scoped("wit", [128, 8, DIP], BF16)
    xa_fm = scoped("xa_fm", [128, 8, TOK], BF16)
    stkB = contextlib.ExitStack()
    def scopedB(name, shape, dtype):
        return stkB.enter_context(nc.sbuf_tensor(name, shape, dtype))
    xt = [scopedB(f"xt{i}", [128, D], F32) for i in range(2)]
    sq_scr = scopedB("sq_scr", [128, D], F32)
    xn_t = [scopedB(f"xn_t{i}", [128, D], BF16) for i in range(2)]
    xa_tmp = scopedB("xa_tmp", [128, D], F32)
    xa_t = [scopedB(f"xa_t{i}", [128, D], BF16) for i in range(2)]
    niw_bc = scopedB("niw_bc", [128, D], F32)
    nib_bc = scopedB("nib_bc", [128, D], F32)
    pA = nc.alloc_psum_tensor("pA", [128, 2048], F32)
    pB = nc.alloc_psum_tensor("pB", [128, 2048], F32)

    ld = P.S("ld")
    gld = P.S("gld")
    gso = P.S("gso")
    sld = P.S("sld")
    so = P.S("so")
    pe = P.S("pe")
    ve = P.S("ve")
    ae = P.S("ae")
    po = P.S("po")

    with nc.Block() as blk:
        @blk.gpsimd
        def _(g):
            for n in CONST_L1:
                gld.dma(g.dma_start(K[n][:], cin[n]))
            gld.dma(g.dma_start(ut2_f[:], cin["ut2"]))
            gld.dma(g.dma_start(ones128_f[:], cin["onescol128"]))
            gld.dma(g.dma_start(sim_f[:], sim_in.rearrange("(c p) e -> p c e", p=128)))
            gld.dma(g.dma_start(sg_row[:], gates_in))
            gld.dma(g.dma_start(alog_row[:], alog_in))
            gld.dma(g.dma_start(wibf_dr, wi_in))
            g.wait_ge(gld.h, gld.n)
            gld.dma(g.dma_start(niw_bc[:], niw_in.broadcast_to([128, D])))
            gld.dma(g.dma_start(nib_bc[:], nib_in.broadcast_to([128, D])))
            gld.dma(g.dma_start(dtb_bc[:], dtb_in.broadcast_to([128, H])))
            g.wait_ge(gld.h, gld.n)
            po.inc(g.tensor_tensor(simsq[:], sim_f[:], sim_f[:], ALU.mult))

        @blk.scalar
        def _(s):
            s.wait_ge(gld.h, gld.n)
            for dc in range(8):
                sld.dma(s.dma_start(wit[:, dc, :], wibf_dr[:, 128 * dc:128 * (dc + 1)],
                                    transpose=True))
            s.activation(a_bc[0:1, :], alog_row[:], AF.Exp)
            ae.inc(s.activation(sg_e[:], sg_row[:], AF.Exp, scale=-1.0))

        @blk.vector
        def _(v):
            v.wait_ge(ae.h, ae.n)
            v.tensor_scalar(a_bc[0:1, :], a_bc[0:1, :], -1.0, None, ALU.mult)
            ve.inc(v.tensor_scalar(sg_e[:], sg_e[:], 1.0, None, ALU.add))
            v.wait_ge(ve.h, ve.n)
            ve.inc(v.reciprocal(sg_r[:], sg_e[:]))

        @blk.gpsimd
        def _(g):
            g.wait_ge(ve.h, ve.n)
            gld.dma(g.dma_start(row_dr[0:1, 0:H], a_bc[0:1, :]))
            gld.dma(g.dma_start(row_dr[0:1, H:H + 16], sg_r[:]))
            g.wait_ge(gld.h, gld.n)
            gld.dma(g.dma_start(a_bc[:], row_dr[0:1, 0:H].broadcast_to([128, H])))
            gld.dma(g.dma_start(sg_bc[:], row_dr[0:1, H:H + 16].broadcast_to([128, 16])))
            g.wait_ge(gld.h, gld.n)
            po.inc(g.nop())
    P.barrier()

    if stop_after <= 0:
        with nc.Block() as blk:
            @blk.sync
            def _(s):
                for ds in (ld, gld, sld, so, gso):
                    s.wait_ge(ds.h, ds.n)
        return P
    b_ld, b_ae, b_ve = ld.n, ae.n, ve.n
    with nc.Block() as blk:
        @blk.sync
        def _(s):
            for i in range(NT):
                if i >= 2:
                    s.wait_ge(ae.h, b_ae + i - 1)
                    s.wait_ge(ve.h, b_ve + 2 * (i - 1))
                ld.dma(s.dma_start(xt[i % 2][:], x_in[128 * i:128 * (i + 1), :]))

        @blk.scalar
        def _(s):
            for i in range(NT):
                s.wait_ge(ld.h, b_ld + 16 * (i + 1))
                if i > 0:
                    s.wait_ge(ve.h, b_ve + 2 * i)
                s.activation(sq_scr[:], xt[i % 2][:], AF.Square).then_inc(ae.h, 1)
                ae.n += 1

        @blk.vector
        def _(v):
            for i in range(NT):
                v.wait_ge(ae.h, b_ae + i + 1)
                v.reduce_sum(ST["s1"][:, i:i + 1], xt[i % 2][:], axis=AX.X,
                             negate=True).then_inc(ve.h, 1)
                ve.n += 1
                v.reduce_sum(ST["s2"][:, i:i + 1], sq_scr[:], axis=AX.X).then_inc(ve.h, 1)
                ve.n += 1
    P.barrier()

    with nc.Block() as blk:
        @blk.vector
        def _(v):
            v.tensor_scalar(ST["mu"][:], ST["s1"][:], 1.0 / D, None, ALU.mult)
            ve.inc(v.tensor_scalar(ST["tmp"][:], ST["s2"][:], 1.0 / D, None, ALU.mult))
            v.wait_ge(ve.h, ve.n)
            ve.inc(v.tensor_tensor(ST["var"][:], ST["mu"][:], ST["mu"][:], ALU.mult))
            v.wait_ge(ve.h, ve.n)
            ve.inc(v.tensor_tensor(ST["var"][:], ST["tmp"][:], ST["var"][:], ALU.subtract))

        @blk.vector
        def _(v):
            v.wait_ge(ve.h, ve.n)
            ve.inc(v.tensor_scalar(ST["tmp"][:], ST["var"][:], EPS, None, ALU.add))

        @blk.scalar
        def _(s):
            s.wait_ge(ve.h, ve.n)
            ae.inc(s.activation(ST["tmp"][:], ST["tmp"][:], AF.Ln))
            s.wait_ge(ae.h, ae.n)
            ae.inc(s.activation(ST["rs"][:], ST["tmp"][:], AF.Exp, scale=-0.5))

        @blk.vector
        def _(v):
            v.wait_ge(ae.h, ae.n)
            ve.inc(v.tensor_tensor(ST["p"][:], ST["var"][:], ST["rs"][:], ALU.mult))
            v.wait_ge(ve.h, ve.n)
            ve.inc(v.tensor_tensor(ST["p"][:], ST["p"][:], ST["rs"][:], ALU.mult))
            v.wait_ge(ve.h, ve.n)
            ve.inc(v.tensor_scalar(ST["tmp"][:], ST["p"][:], EPS, None, ALU.add))

        @blk.scalar
        def _(s):
            s.wait_ge(ve.h, ve.n)
            ae.inc(s.activation(ST["tmp"][:], ST["tmp"][:], AF.Ln))
            s.wait_ge(ae.h, ae.n)
            ae.inc(s.activation(ST["al"][:], ST["tmp"][:], AF.Exp, scale=-0.5))

        @blk.vector
        def _(v):
            v.wait_ge(ae.h, ae.n)
            ve.inc(v.tensor_scalar(ST["tmp"][:], ST["p"][:], 1e-12, None, ALU.max))

        @blk.scalar
        def _(s):
            s.wait_ge(ve.h, ve.n)
            ae.inc(s.activation(ST["tmp"][:], ST["tmp"][:], AF.Ln))
            s.wait_ge(ae.h, ae.n)
            ae.inc(s.activation(ST["rn"][:], ST["tmp"][:], AF.Exp, scale=-0.5))

        @blk.vector
        def _(v):
            v.wait_ge(ae.h, ae.n)
            v.tensor_scalar(ST["rn"][:], ST["rn"][:], 1.0 / 32.0, None, ALU.mult)
            ve.inc(v.tensor_tensor(ST["rsp"][:], ST["rs"][:], ST["al"][:], ALU.mult))
            v.wait_ge(ve.h, ve.n)
            v.tensor_tensor(ST["bxn"][:], ST["mu"][:], ST["rs"][:], ALU.mult)
            ve.inc(v.tensor_tensor(ST["bxa"][:], ST["mu"][:], ST["rsp"][:], ALU.mult))
    P.barrier()

    b_ld, b_ve, b_so = ld.n, ve.n, so.n
    with nc.Block() as blk:
        @blk.sync
        def _(s):
            for i in range(NT):
                if i >= 2:
                    s.wait_ge(ve.h, b_ve + 2 * (i - 1))
                ld.dma(s.dma_start(xt[i % 2][:], x_in[128 * i:128 * (i + 1), :]))

        @blk.vector
        def _(v):
            for i in range(NT):
                v.wait_ge(ld.h, b_ld + 16 * (i + 1))
                if i >= 2:
                    v.wait_ge(so.h, b_so + 32 * (i - 1))
                v.tensor_scalar(xn_t[i % 2][:], xt[i % 2][:], ST["rs"][:, i:i + 1],
                                ST["bxn"][:, i:i + 1], ALU.mult, ALU.add).then_inc(ve.h, 1)
                v.tensor_scalar(xa_tmp[:], xt[i % 2][:], ST["rsp"][:, i:i + 1],
                                ST["bxa"][:, i:i + 1], ALU.mult, ALU.add)
                v.tensor_tensor(xa_tmp[:], xa_tmp[:], niw_bc[:], ALU.mult)
                v.tensor_tensor(xa_t[i % 2][:], xa_tmp[:], nib_bc[:],
                                ALU.add).then_inc(ve.h, 1)
                ve.n += 2

        @blk.scalar
        def _(s):
            for i in range(NT):
                s.wait_ge(ve.h, b_ve + 2 * (i + 1))
                so.dma(s.dma_start(xn_dr[128 * i:128 * (i + 1), :], xn_t[i % 2][:]))
                so.dma(s.dma_start(xa_dr[128 * i:128 * (i + 1), :], xa_t[i % 2][:]))
    P.barrier()

    with nc.Block() as blk:
        @blk.scalar
        def _(s):
            s.wait_ge(so.h, so.n)
            for dc in range(8):
                sld.dma(s.dma_start(xn_fm[:, dc, :], xn_dr[:, 128 * dc:128 * (dc + 1)],
                                    transpose=True))
                sld.dma(s.dma_start(xa_fm[:, dc, :], xa_dr[:, 128 * dc:128 * (dc + 1)],
                                    transpose=True))
    P.barrier()

    if DEBUG_OUT:
        o_dbg_xn = nc.dram_tensor("o_dbg_xn", [128, 8 * TOK], F32, kind="ExternalOutput").ap()
        o_dbg_xa = nc.dram_tensor("o_dbg_xa", [128, 8 * TOK], F32, kind="ExternalOutput").ap()
        o_dbg_xadr = nc.dram_tensor("o_dbg_xadr", [TOK, D], F32, kind="ExternalOutput").ap()
        o_dbg_st = nc.dram_tensor("o_dbg_st", [128, 4 * NT], F32, kind="ExternalOutput").ap()
        o_dbg_wit = nc.dram_tensor("o_dbg_wit", [128, 8 * DIP], F32, kind="ExternalOutput").ap()
        o_dbg_nb = nc.dram_tensor("o_dbg_nb", [128, 2 * D], F32, kind="ExternalOutput").ap()
        with nc.Block() as blk:
            @blk.gpsimd
            def _(g):
                gso.dma(g.dma_start(o_dbg_xn.rearrange("p (c t) -> p c t", c=8), xn_fm[:]))
                gso.dma(g.dma_start(o_dbg_xa.rearrange("p (c t) -> p c t", c=8), xa_fm[:]))
                gso.dma(g.dma_start(o_dbg_xadr, xa_dr))
                gso.dma(g.dma_start(o_dbg_st[:, 0:NT], ST["s1"][:]))
                gso.dma(g.dma_start(o_dbg_st[:, NT:2 * NT], ST["s2"][:]))
                gso.dma(g.dma_start(o_dbg_st[:, 2 * NT:3 * NT], ST["rs"][:]))
                gso.dma(g.dma_start(o_dbg_st[:, 3 * NT:4 * NT], ST["var"][:]))
                gso.dma(g.dma_start(o_dbg_wit.rearrange("p (c t) -> p c t", c=8), wit[:]))
                gso.dma(g.dma_start(o_dbg_nb[:, 0:D], niw_bc[:]))
                gso.dma(g.dma_start(o_dbg_nb[:, D:2 * D], nib_bc[:]))
                g.wait_ge(gso.h, gso.n)
        P.barrier()

    if stop_after <= 1:
        with nc.Block() as blk:
            @blk.sync
            def _(s):
                for ds in (ld, gld, sld, so, gso):
                    s.wait_ge(ds.h, ds.n)
        return P
    stkB.close()
    stkC = contextlib.ExitStack()
    def scopedC(name, shape, dtype):
        return stkC.enter_context(nc.sbuf_tensor(name, shape, dtype))
    stage = [scopedC(f"stage{i}", [128, DI], F32) for i in range(2)]
    xs_bf = [scopedC(f"xs_bf{i}", [128, DI], BF16) for i in range(2)]
    cfm_st = scopedC("cfm_st", [64, TOK], F32)
    b_pe, b_ae, b_ve, b_so = pe.n, ae.n, ve.n, so.n
    with nc.Block() as blk:
        @blk.tensor
        def _(t):
            for i in range(NT):
                t0 = 128 * i
                if i > 0:
                    t.wait_ge(ae.h, b_ae + 4 * (i - 1) + 4)
                    t.wait_ge(ve.h, b_ve + 2 * i)
                for dc in range(8):
                    for fg in range(4):
                        mm = t.matmul(pA[:, 512 * fg:512 * (fg + 1)],
                                 xa_fm[:, dc, t0:t0 + 128],
                                 wit[:, dc, 512 * fg:512 * (fg + 1)],
                                 start=(dc == 0), stop=(dc == 7))
                pe.inc(mm)
                if i > 0:
                    t.wait_ge(ve.h, b_ve + 2 * (i - 1) + 1)
                for dc in range(8):
                    for fg in range(4):
                        mm = t.matmul(pB[:, 512 * fg:512 * (fg + 1)],
                                 xa_fm[:, dc, t0:t0 + 128],
                                 wit[:, dc, 2048 + 512 * fg:2048 + 512 * (fg + 1)],
                                 start=(dc == 0), stop=(dc == 7))
                pe.inc(mm)
                t.wait_ge(ae.h, b_ae + 4 * i + 4)
                for dc in range(8):
                    mm = t.matmul(pA[:, 0:160], xa_fm[:, dc, t0:t0 + 128],
                             wit[:, dc, 4096:4256], start=(dc == 0), stop=(dc == 7))
                pe.inc(mm)

        @blk.scalar
        def _(s):
            for i in range(NT):
                s.wait_ge(pe.h, b_pe + 3 * i + 1)
                if i >= 2:
                    s.wait_ge(so.h, b_so + 32 * (i - 2) + 16)
                for fg in range(4):
                    s.activation(stage[i % 2][:, 512 * fg:512 * (fg + 1)],
                                 pA[:, 512 * fg:512 * (fg + 1)], AF.Copy).then_inc(ae.h, 1)
                    ae.n += 1

        @blk.vector
        def _(v):
            for i in range(NT):
                v.wait_ge(pe.h, b_pe + 3 * i + 2)
                if i >= 2:
                    v.wait_ge(so.h, b_so + 32 * (i - 2) + 32)
                v.tensor_copy(xs_bf[i % 2][:], pB[:]).then_inc(ve.h, 1)
                ve.n += 1
                v.wait_ge(pe.h, b_pe + 3 * i + 3)
                v.tensor_copy(bt_bf[:, i, :], pA[:, 0:64])
                v.tensor_copy(dt_f[:, i, :], pA[:, 128:160]).then_inc(ve.h, 1)
                ve.n += 1

        @blk.sync
        def _(s):
            for i in range(NT):
                s.wait_ge(ae.h, b_ae + 4 * i + 4)
                so.dma(s.dma_start(o_z[128 * i:128 * (i + 1), :], stage[i % 2][:]))
                s.wait_ge(ve.h, b_ve + 2 * i + 1)
                so.dma(s.dma_start(xs_dr[128 * i:128 * (i + 1), :], xs_bf[i % 2][:]))
    P.barrier()

    b_pe, b_ve = pe.n, ve.n
    b_so_all = so.n
    with nc.Block() as blk:
        @blk.tensor
        def _(t):
            for tc in range(4):
                if tc > 0:
                    t.wait_ge(ve.h, b_ve + 3 * tc)
                for dc in range(8):
                    mm = t.matmul(pA[:, 0:512], wit[:, dc, 4096:4224],
                             xa_fm[:, dc, 512 * tc:512 * (tc + 1)],
                             start=(dc == 0), stop=(dc == 7))
                pe.inc(mm)

        @blk.vector
        def _(v):
            v.wait_ge(so.h, b_so_all)
            for tc in range(4):
                v.wait_ge(pe.h, b_pe + tc + 1)
                v.tensor_copy(stage[0][:, 0:512], pA[:, 0:512]).then_inc(ve.h, 1)
                v.tensor_copy(bfm_bf[:, 512 * tc:512 * (tc + 1)], stage[0][0:64, 0:512])
                v.tensor_copy(cfm_bf[:, 512 * tc:512 * (tc + 1)],
                              stage[0][64:128, 0:512]).then_inc(ve.h, 1)
                v.tensor_copy(cfm_st[:, 512 * tc:512 * (tc + 1)],
                              stage[0][64:128, 0:512]).then_inc(ve.h, 1)
                ve.n += 3

        @blk.sync
        def _(s):
            s.wait_ge(ve.h, ve.n)
            so.dma(s.dma_start(o_cfm, cfm_st[:]))
    P.barrier()

    if stop_after <= 3:
        with nc.Block() as blk:
            @blk.sync
            def _(s):
                for ds in (ld, gld, sld, so, gso):
                    s.wait_ge(ds.h, ds.n)
        return P
    stkD = contextlib.ExitStack()
    DT = {n: stkD.enter_context(nc.sbuf_tensor("dts_" + n, [128, NT * H], F32))
          for n in ("logit", "l2", "t1", "fb")}
    dtf_all = dt_f[:].rearrange("p a b -> p (a b)")
    la_all = la_hi[:].rearrange("p a b -> p (a b)")
    with nc.Block() as blk:
        @blk.vector
        def _(v):
            ve.inc(v.tensor_tensor(dt_f[:], dt_f[:],
                                   dtb_bc[:].unsqueeze(1).broadcast_to([128, NT, H]),
                                   ALU.add))

        @blk.scalar
        def _(s):
            s.wait_ge(ve.h, ve.n)
            ae.inc(s.activation(DT["logit"][:], dtf_all, AF.Abs))
            s.wait_ge(ae.h, ae.n)
            ae.inc(s.activation(DT["l2"][:], DT["logit"][:], AF.Exp, scale=-1.0))

        @blk.vector
        def _(v):
            v.wait_ge(ae.h, ae.n)
            ve.inc(v.tensor_scalar(DT["l2"][:], DT["l2"][:], 1.0, None, ALU.add))

        @blk.scalar
        def _(s):
            s.wait_ge(ve.h, ve.n)
            ae.inc(s.activation(DT["t1"][:], DT["l2"][:], AF.Ln))

        @blk.vector
        def _(v):
            v.tensor_scalar(DT["fb"][:], dtf_all, 0.0, None, ALU.max)
            v.wait_ge(ae.h, ae.n)
            ve.inc(v.tensor_tensor(DT["fb"][:], DT["fb"][:], DT["t1"][:], ALU.add))
            v.wait_ge(ve.h, ve.n)
            ve.inc(v.tensor_scalar(DT["fb"][:], DT["fb"][:], 0.001, 0.1, ALU.max, ALU.min))
            v.wait_ge(ve.h, ve.n)
            fb3 = DT["fb"][:].rearrange("p (a b) -> p a b", b=H)
            ve.inc(v.tensor_tensor(fb3, fb3,
                                   a_bc[:].unsqueeze(1).broadcast_to([128, NT, H]),
                                   ALU.mult))
            v.wait_ge(ve.h, ve.n)
            ve.inc(v.tensor_scalar(DT["fb"][:], DT["fb"][:], LOG99, None, ALU.min))
            v.wait_ge(ve.h, ve.n)
            ve.inc(v.tensor_copy(la_all, DT["fb"][:]))
    P.barrier()

    if DEBUG_OUT:
        o_dbg_la = nc.dram_tensor("o_dbg_la", [128, NT * H], F32, kind="ExternalOutput").ap()
        o_dbg_dt = nc.dram_tensor("o_dbg_dt", [128, NT * H], F32, kind="ExternalOutput").ap()
        o_dbg_bt = nc.dram_tensor("o_dbg_bt", [128, NT * 64], F32, kind="ExternalOutput").ap()
        with nc.Block() as blk:
            @blk.gpsimd
            def _(g):
                gso.dma(g.dma_start(o_dbg_la.rearrange("p (c t) -> p c t", c=NT), la_hi[:]))
                gso.dma(g.dma_start(o_dbg_dt.rearrange("p (c t) -> p c t", c=NT), dt_f[:]))
                gso.dma(g.dma_start(o_dbg_bt.rearrange("p (c t) -> p c t", c=NT), bt_bf[:]))
                g.wait_ge(gso.h, gso.n)
        P.barrier()

    if stop_after <= 4:
        with nc.Block() as blk:
            @blk.sync
            def _(s):
                for ds in (ld, gld, sld, so, gso):
                    s.wait_ge(ds.h, ds.n)
        return P
    stkD.close()
    stkC.close()
    stk.close()
    stkS = contextlib.ExitStack()
    def scopedS(name, shape, dtype):
        return stkS.enter_context(nc.sbuf_tensor(name, shape, dtype))
    x_pair = [scopedS(f"x_pair{i}", [128, DI], BF16) for i in range(2)]
    cum_sb = scopedS("cum_sb", [128, H], F32)
    nch = scopedS("nch", [128, H], BF16)
    ncl_t = scopedS("ncl_t", [128, H], F32)
    ncl = scopedS("ncl", [128, H], BF16)
    e_pair = scopedS("e_pair", [128, H], F32)
    d_pair = scopedS("d_pair", [128, H], F32)
    lm_hi = scopedS("lm_hi", [128, H, QC], BF16)
    m_bf = scopedS("m_bf", [128, H, QC], BF16)
    gc_bf = scopedS("gc_bf", [128, QC], BF16)
    mg_bf = scopedS("mg_bf", [128, H, QC], BF16)
    tot_sb = scopedS("tot_sb", [1, 2 * H], F32)
    pd_bc = scopedS("pd_bc", [64, 2 * H], F32)
    csp = scopedS("csp", [128, H], F32)
    sdec_acc = scopedS("sdec_acc", [1, 2 * H], F32)
    cs_st = [scopedS(f"cs_st{i}", [128, H], F32) for i in range(2)]
    ex_full = scopedS("ex_full", [128, DI], BF16)
    yint = scopedS("yint", [128, DI], BF16)
    hsc = scopedS("hsc", [64, DI], BF16)
    h_st = [scopedS(f"h_st{i}", [64, DI], BF16) for i in range(2)]
    xdp = [scopedS(f"xdp{i}", [128, DI], BF16) for i in range(2)]
    yst = [scopedS(f"yst{i}", [128, DI], F32) for i in range(2)]
    dpf_bc = scopedS("dpf_bc", [128, DI], F32)
    with nc.Block() as blk:
        @blk.gpsimd
        def _(g):
            gld.dma(g.dma_start(dpf_bc[:], dpf_in.broadcast_to([128, DI])))
            g.memset(h_st[0][:], 0.0)
            po.inc(g.memset(sdec_acc[:], 0.0))
            g.wait_ge(gld.h, gld.n)
            po.inc(g.nop())
    P.barrier()
    scan_ld = P.S("scan_ld")
    sve = P.S("sve")
    so_pair = []

    with nc.Block() as blk:
        @blk.sync
        def _(s):
            for gi in range(2):
                scan_ld.dma(s.dma_start(x_pair[gi % 2][:],
                                        xs_dr[128 * gi:128 * (gi + 1), :]))

    for gi in range(scan_pairs):
        sp, p = divmod(gi, 8)
        ti = gi
        c0 = 1024 * sp + 128 * p
        with nc.Block() as blk:
            @blk.tensor
            def _(t, gi=gi, ti=ti):
                if SCAN_STAGE < 1: return
                t.wait_ge(sve.h, sve.n)
                t.wait_ge(ae.h, ae.n)
                t.matmul(pA[:, 0:32], K["bdut"][:], la_hi[:, ti, :], start=True, stop=True)
                t.matmul(pA[:, 64:96], K["bdsl"][:], la_hi[:, ti, :], start=True, stop=True)
                t.matmul(pA[0:64, 96:128], K["onesA64"][:], la_hi[:, ti, :],
                         start=True, stop=True)
                t.matmul(pA[0:64, 128:160], K["onesB64"][:], la_hi[:, ti, :],
                         start=True, stop=True)
                sp0, p0 = divmod(gi, 8)
                for pp in range(p0):
                    t.matmul(pA[:, 160:192], K["ones128sq"][:],
                             la_hi[:, 8 * sp0 + pp, :], start=(pp == 0), stop=False)
                mm = t.matmul(pA[:, 160:192], K["oneshalf"][:], la_hi[:, ti, :],
                         start=(p0 == 0), stop=True)
                pe.inc(mm)

            @blk.scalar
            def _(s, gi=gi):
                if SCAN_STAGE < 2: return
                s.wait_ge(pe.h, pe.n)
                s.activation(cum_sb[:], pA[:, 0:32], AF.Copy)
                s.activation(d_pair[:], pA[:, 0:32], AF.Exp)
                s.activation(e_pair[:], pA[:, 64:96], AF.Exp)
                s.activation(tot_sb[:], pA[0:1, 96:160], AF.Copy)
                s.activation(csp[:], pA[:, 160:192], AF.Copy)
                ae.inc(s.activation(pd_bc[:], pA[0:64, 96:160], AF.Exp))

            @blk.vector
            def _(v, gi=gi, ti=ti):
                if SCAN_STAGE < 3: return
                v.wait_ge(ae.h, ae.n)
                ve.inc(v.tensor_scalar(nch[:], cum_sb[:], -1.0, None, ALU.mult))
                v.wait_ge(ve.h, ve.n)
                ve.inc(v.tensor_tensor(ncl_t[:], cum_sb[:], nch[:], ALU.add))
                v.wait_ge(ve.h, ve.n)
                v.tensor_scalar(ncl[:], ncl_t[:], -1.0, None, ALU.mult)
                v.tensor_tensor(
                    lm_hi[:], K["ut2"][:].unsqueeze(1).broadcast_to([128, H, QC]),
                    la_hi[:, ti, :].unsqueeze(2).broadcast_to([128, H, QC]),
                    ALU.mult)
                v.tensor_tensor(sdec_acc[0:1, 0:H], sdec_acc[0:1, 0:H],
                                tot_sb[0:1, 0:H], ALU.add)
                v.tensor_tensor(sdec_acc[0:1, H:2 * H], sdec_acc[0:1, H:2 * H],
                                tot_sb[0:1, H:2 * H], ALU.add).then_inc(ve.h, 1)
                ve.n += 1

            @blk.gpsimd
            def _(g, gi=gi):
                if SCAN_STAGE < 4: return
                g.wait_ge(ve.h, ve.n)
                g.wait_ge(scan_ld.h, 16 * (gi + 1))
                g.tensor_tensor(ex_full[:].rearrange("p (h k) -> p h k", k=PH),
                                x_pair[gi % 2][:].rearrange("p (h k) -> p h k", k=PH),
                                e_pair[:].unsqueeze(2).broadcast_to([128, H, PH]),
                                ALU.mult)
                po.inc(g.tensor_tensor(xdp[gi % 2][:], x_pair[gi % 2][:], dpf_bc[:],
                                       ALU.mult))

            @blk.vector
            def _(v, gi=gi):
                if SCAN_STAGE < 5: return
                v.wait_ge(po.h, po.n)
                v.wait_ge(ae.h, ae.n)
                v.tensor_tensor(cs_st[gi % 2][:], cum_sb[:], csp[:], ALU.add)
                v.tensor_tensor(hsc[:].rearrange("p (h k) -> p h k", k=PH),
                                h_st[0][:].rearrange("p (h k) -> p h k", k=PH),
                                pd_bc[:, 0:H].unsqueeze(2).broadcast_to([64, H, PH]),
                                ALU.mult).then_inc(ve.h, 1)
                ve.n += 1

            @blk.tensor
            def _(t, gi=gi, ti=ti):
                if SCAN_STAGE < 6: return
                t.wait_ge(ve.h, ve.n)
                for gr in range(4):
                    view = pA[:, 512 * gr:512 * (gr + 1)].rearrange("p (h q) -> p h q", q=QC)
                    t.matmul(view, K["bdones"][:], lm_hi[:, 8 * gr:8 * (gr + 1), :],
                             start=True, stop=False)
                for gr in range(4):
                    view = pA[:, 512 * gr:512 * (gr + 1)].rearrange("p (h q) -> p h q", q=QC)
                    t.matmul(view, K["eye128"][:],
                             nch[:, 8 * gr:8 * (gr + 1)].unsqueeze(2).broadcast_to([128, 8, QC]),
                             start=False, stop=False)
                    t.matmul(view, K["eye128"][:],
                             ncl[:, 8 * gr:8 * (gr + 1)].unsqueeze(2).broadcast_to([128, 8, QC]),
                             start=False, stop=False)
                    mm = t.matmul(view, K["eye128"][:],
                             K["cn2"][:].unsqueeze(1).broadcast_to([128, 8, QC]),
                             start=False, stop=True)
                pe.inc(mm)
                for fg in range(4):
                    t.matmul(pB[0:64, 512 * fg:512 * (fg + 1)], bt_bf[0:64, ti, :],
                             ex_full[0:64, 512 * fg:512 * (fg + 1)], start=True, stop=False)
                for fg in range(4):
                    mm = t.matmul(pB[0:64, 512 * fg:512 * (fg + 1)], K["eye64"][:],
                             hsc[:, 512 * fg:512 * (fg + 1)], start=False, stop=True)
                pe.inc(mm)

            @blk.scalar
            def _(s, gi=gi):
                if SCAN_STAGE < 7: return
                s.wait_ge(pe.h, pe.n - 1)
                s.activation(m_bf[:], pA[:].rearrange("p (h q) -> p h q", q=QC),
                             AF.Exp).then_inc(ae.h, 1)
                ae.n += 1
                s.wait_ge(pe.h, pe.n)
                ae.inc(s.activation(h_st[1][:], pB[0:64, :], AF.Copy))

            @blk.tensor
            def _(t, gi=gi, c0=c0):
                if SCAN_STAGE < 8: return
                t.wait_ge(ae.h, ae.n - 1)
                t.matmul(pA[0:64, 0:64], bfm_bf[:, c0:c0 + 64], cfm_bf[:, c0:c0 + 64],
                         start=True, stop=True)
                mm = t.matmul(pA[64:128, 0:64], bfm_bf[:, c0 + 64:c0 + 128],
                         cfm_bf[:, c0 + 64:c0 + 128], start=True, stop=True,
                         tile_position=(0, 64))
                pe.inc(mm)

            @blk.vector
            def _(v, gi=gi):
                if SCAN_STAGE < 9: return
                v.wait_ge(pe.h, pe.n)
                ve.inc(v.tensor_tensor(gc_bf[:], pA[:, 0:64], ut2_f[:], ALU.mult))
                v.wait_ge(ve.h, ve.n)
                v.tensor_tensor(mg_bf[:], m_bf[:],
                                gc_bf[:].unsqueeze(1).broadcast_to([128, H, QC]),
                                ALU.mult).then_inc(ve.h, 1)
                ve.n += 1

            @blk.tensor
            def _(t, gi=gi, c0=c0):
                if SCAN_STAGE < 10: return
                t.wait_ge(ve.h, ve.n)
                for fg in range(4):
                    mm = t.matmul(pA[0:64, 512 * fg:512 * (fg + 1)], cfm_bf[:, c0:c0 + 64],
                             h_st[0][:, 512 * fg:512 * (fg + 1)], start=True, stop=True)
                pe.inc(mm)

            @blk.vector
            def _(v, gi=gi):
                if SCAN_STAGE < 11: return
                v.wait_ge(ae.h, ae.n)
                v.tensor_tensor(hsc[:].rearrange("p (h k) -> p h k", k=PH),
                                h_st[1][:].rearrange("p (h k) -> p h k", k=PH),
                                pd_bc[:, H:2 * H].unsqueeze(2).broadcast_to([64, H, PH]),
                                ALU.mult).then_inc(ve.h, 1)
                ve.n += 1

            @blk.tensor
            def _(t, gi=gi, ti=ti, c0=c0):
                if SCAN_STAGE < 12: return
                t.wait_ge(ve.h, ve.n)
                for fg in range(4):
                    mm = t.matmul(pA[64:128, 512 * fg:512 * (fg + 1)],
                             cfm_bf[:, c0 + 64:c0 + 128],
                             h_st[1][:, 512 * fg:512 * (fg + 1)], start=True, stop=True,
                             tile_position=(0, 64))
                pe.inc(mm)
                for fg in range(4):
                    mm = t.matmul(pB[0:64, 512 * fg:512 * (fg + 1)], bt_bf[64:128, ti, :],
                             ex_full[64:128, 512 * fg:512 * (fg + 1)], start=True, stop=True,
                             tile_position=(64, 0))
                pe.inc(mm)

            @blk.vector
            def _(v, gi=gi):
                if SCAN_STAGE < 13: return
                v.wait_ge(pe.h, pe.n - 1)
                v.tensor_tensor(yint[:].rearrange("p (h k) -> p h k", k=PH),
                                pA[:].rearrange("p (h k) -> p h k", k=PH),
                                d_pair[:].unsqueeze(2).broadcast_to([128, H, PH]),
                                ALU.mult).then_inc(ve.h, 1)
                ve.n += 1

            @blk.vector
            def _(v, gi=gi):
                if SCAN_STAGE < 14: return
                v.wait_ge(pe.h, pe.n)
                ve.inc(v.tensor_tensor(h_st[0][:], pB[0:64, :], hsc[:], ALU.add))

            @blk.tensor
            def _(t, gi=gi):
                if SCAN_STAGE < 15: return
                t.wait_ge(ve.h, ve.n)
                for fg in range(4):
                    t.matmul(pA[:, 512 * fg:512 * (fg + 1)], K["eye128"][:],
                             xdp[gi % 2][:, 512 * fg:512 * (fg + 1)], start=True, stop=False)
                for h in range(H):
                    t.matmul(pA[0:64, 64 * h:64 * (h + 1)], mg_bf[0:64, h, :],
                             x_pair[gi % 2][0:64, 64 * h:64 * (h + 1)],
                             start=False, stop=False)
                for fg in range(4):
                    t.matmul(pA[:, 512 * fg:512 * (fg + 1)], K["eye128"][:],
                             yint[:, 512 * fg:512 * (fg + 1)], start=False, stop=True)
                for fg in range(4):
                    t.matmul(pB[64:128, 512 * fg:512 * (fg + 1)], K["eyeB64"][64:128, :],
                             xdp[gi % 2][64:128, 512 * fg:512 * (fg + 1)],
                             start=True, stop=False, tile_position=(64, 64))
                for h in range(H):
                    t.matmul(pB[64:128, 64 * h:64 * (h + 1)], mg_bf[64:128, h, :],
                             x_pair[gi % 2][64:128, 64 * h:64 * (h + 1)],
                             start=False, stop=False, tile_position=(64, 64))
                for fg in range(4):
                    mm = t.matmul(pB[64:128, 512 * fg:512 * (fg + 1)], K["eyeB64"][64:128, :],
                             yint[64:128, 512 * fg:512 * (fg + 1)],
                             start=False, stop=True, tile_position=(64, 64))
                pe.inc(mm)

            @blk.vector
            def _(v, gi=gi, p=p):
                if SCAN_STAGE < 16: return
                v.wait_ge(pe.h, pe.n)
                if gi >= 2:
                    v.wait_ge(so.h, so_pair[gi - 2])
                if p == 7:
                    v.tensor_tensor(sdec_acc[0:1, 0:H], sdec_acc[0:1, 0:H],
                                    sdec_acc[0:1, H:2 * H], ALU.add)
                v.tensor_copy(yst[gi % 2][0:64, :], pA[0:64, :])
                sve.inc(v.tensor_copy(yst[gi % 2][64:128, :], pB[64:128, :]))

            @blk.sync
            def _(s, gi=gi, sp=sp, p=p):
                if SCAN_STAGE < 17: return
                s.wait_ge(sve.h, sve.n)
                if gi + 2 < 16:
                    scan_ld.dma(s.dma_start(x_pair[gi % 2][:],
                                            xs_dr[128 * (gi + 2):128 * (gi + 3), :]))
                so.dma(s.dma_start(o_yp[128 * gi:128 * (gi + 1), :], yst[gi % 2][:]))
                so.dma(s.dma_start(o_cs[128 * gi:128 * (gi + 1), :], cs_st[gi % 2][:]))
                if p == 7:
                    so.dma(s.dma_start(o_sdec[0:1, H * sp:H * (sp + 1)],
                                       sdec_acc[0:1, 0:H]))
        if p == 7:
            with nc.Block() as blkh:
                @blkh.gpsimd
                def _(g, sp=sp):
                    g.wait_ge(sve.h, sve.n)
                    gso.dma(g.dma_start(o_hend[64 * sp:64 * (sp + 1), :], h_st[0][:]))
        so_pair.append(so.n)
        if p == 7 and sp == 0:
            with nc.Block() as blk:
                @blk.gpsimd
                def _(g):
                    g.wait_ge(so.h, so.n)
                    g.wait_ge(gso.h, gso.n)
                    g.memset(h_st[0][:], 0.0)
                    g.memset(sdec_acc[:], 0.0)
                    po.inc(g.nop())

                @blk.vector
                def _(v):
                    v.wait_ge(po.h, po.n)
                    sve.inc(v.nop())
    P.barrier()

    if stop_after <= 5:
        with nc.Block() as blk:
            @blk.sync
            def _(s):
                for ds in (ld, gld, sld, so, gso):
                    s.wait_ge(ds.h, ds.n)
        return P
    stkS.close()
    stkQ = contextlib.ExitStack()
    def scopedQ(name, shape, dtype):
        return stkQ.enter_context(nc.sbuf_tensor(name, shape, dtype))
    wqkv = scopedQ("wqkv", [128, 8, 16, 192], BF16)
    ssum = scopedQ("ssum", [1, 16], F32)
    sln = scopedQ("sln", [1, 16], F32)
    srec = scopedQ("srec", [1, 16], F32)
    srec_bc = scopedQ("srec_bc", [128, 16], F32)
    LG = {n: scopedQ("lg_" + n, [128, NT, 16], F32) for n in
          ("logit", "l2", "fb", "act", "maskf", "gated", "ex", "em", "t1", "t3")}
    R1 = {n: scopedQ("r1_" + n, [128, NT], F32) for n in
          ("m1", "m2", "any", "use", "iuse", "nmx", "sum", "rin")}
    rw_all = scopedQ("rw_all", [128, NT, 16], F32)
    acc = [scopedQ(f"acc{i}", [128, 192], F32) for i in range(2)]
    qst = [scopedQ(f"qst{i}", [128, 192], F32) for i in range(2)]
    with nc.Block() as blk:
        @blk.gpsimd
        def _(g):
            for dc in range(8):
                dsl = wq_in[:, 128 * dc:128 * (dc + 1), :].rearrange("e p k -> p e k")
                gld.dma(g.dma_start(wqkv[:, dc, :, 0:64], dsl))
                dsl = wk_in[:, 128 * dc:128 * (dc + 1), :].rearrange("e p k -> p e k")
                gld.dma(g.dma_start(wqkv[:, dc, :, 64:128], dsl))
                dsl = wv_in[:, 128 * dc:128 * (dc + 1), :].rearrange("e p k -> p e k")
                gld.dma(g.dma_start(wqkv[:, dc, :, 128:192], dsl))
            g.wait_ge(gld.h, gld.n)
            po.inc(g.nop())
    P.barrier()
    b_pe, b_ae, b_ve, b_po = pe.n, ae.n, ve.n, po.n
    with nc.Block() as blk:
        @blk.tensor
        def _(t):
            for dc in range(8):
                mm = t.matmul(pA[0:64, 0:16], ones128_f[:], simsq[:, dc, :],
                         start=(dc == 0), stop=(dc == 7))
            pe.inc(mm)

        @blk.scalar
        def _(s):
            s.wait_ge(pe.h, pe.n)
            ae.inc(s.activation(ssum[:], pA[0:1, 0:16], AF.Copy))

        @blk.vector
        def _(v):
            v.wait_ge(ae.h, ae.n)
            ve.inc(v.tensor_scalar(ssum[:], ssum[:], 1e-24, None, ALU.max))

        @blk.scalar
        def _(s):
            s.wait_ge(ve.h, ve.n)
            ae.inc(s.activation(sln[:], ssum[:], AF.Ln))
            s.wait_ge(ae.h, ae.n)
            ae.inc(s.activation(srec[:], sln[:], AF.Exp, scale=-0.5))

        @blk.gpsimd
        def _(g):
            g.wait_ge(ae.h, ae.n)
            gld.dma(g.dma_start(row_dr[0:1, 0:16], srec[:]))
            g.wait_ge(gld.h, gld.n)
            gld.dma(g.dma_start(srec_bc[:], row_dr[0:1, 0:16].broadcast_to([128, 16])))
            g.wait_ge(gld.h, gld.n)
            po.inc(g.nop())

        @blk.vector
        def _(v):
            v.wait_ge(po.h, po.n)
            for dc in range(8):
                inst = v.tensor_tensor(sim_nbf[:, dc, :], sim_f[:, dc, :], srec_bc[:],
                                       ALU.mult)
            ve.inc(inst)
    P.barrier()

    lg3 = {n: LG[n][:] for n in LG}
    lg2 = {n: LG[n][:].rearrange("p a b -> p (a b)") for n in LG}
    with nc.Block() as blk:
        @blk.tensor
        def _(t):
            for i in range(NT):
                t0 = 128 * i
                for dc in range(8):
                    mm = t.matmul(pB[:, 1536 + 16 * i:1552 + 16 * i],
                             xn_fm[:, dc, t0:t0 + 128],
                             sim_nbf[:, dc, :], start=(dc == 0), stop=(dc == 7))
            pe.inc(mm)

        @blk.vector
        def _(v):
            v.wait_ge(pe.h, pe.n)
            ve.inc(v.tensor_tensor(
                lg3["logit"], pB[:, 1536:1792].rearrange("p (a b) -> p a b", b=16),
                ST["rn"][:].unsqueeze(2).broadcast_to([128, NT, 16]), ALU.mult))
            v.wait_ge(ve.h, ve.n)
            ve.inc(v.tensor_tensor(
                lg3["logit"], lg3["logit"],
                sg_bc[:].unsqueeze(1).broadcast_to([128, NT, 16]), ALU.subtract))
            v.wait_ge(ve.h, ve.n)
            ve.inc(v.reduce_max(R1["m1"][:].unsqueeze(2), lg3["logit"], axis=AX.X))
            v.wait_ge(ve.h, ve.n)
            ve.inc(v.tensor_tensor(
                lg3["t1"], lg3["logit"],
                R1["m1"][:].unsqueeze(2).broadcast_to([128, NT, 16]), ALU.is_equal))
            v.wait_ge(ve.h, ve.n)
            ve.inc(v.tensor_scalar(lg2["t1"], lg2["t1"], -1e30, None, ALU.mult))
            v.wait_ge(ve.h, ve.n)
            ve.inc(v.tensor_tensor(lg2["l2"], lg2["logit"], lg2["t1"], ALU.add))
            v.wait_ge(ve.h, ve.n)
            ve.inc(v.reduce_max(R1["m2"][:].unsqueeze(2), lg3["l2"], axis=AX.X))
            v.wait_ge(ve.h, ve.n)
            ve.inc(v.tensor_tensor(
                lg3["fb"], lg3["logit"],
                R1["m2"][:].unsqueeze(2).broadcast_to([128, NT, 16]), ALU.is_ge))
            ve.inc(v.tensor_scalar(lg2["act"], lg2["logit"], 0.0, None, ALU.is_gt))
            v.wait_ge(ve.h, ve.n)
            ve.inc(v.reduce_sum(R1["any"][:].unsqueeze(2), lg3["act"], axis=AX.X))
            v.wait_ge(ve.h, ve.n)
            ve.inc(v.tensor_scalar(R1["use"][:], R1["any"][:], 0.5, None, ALU.is_gt))
            v.wait_ge(ve.h, ve.n)
            ve.inc(v.tensor_scalar(R1["iuse"][:], R1["use"][:], -1.0, 1.0,
                                   ALU.mult, ALU.add))
            v.wait_ge(ve.h, ve.n)
            v.tensor_tensor(lg3["t1"], lg3["act"],
                            R1["use"][:].unsqueeze(2).broadcast_to([128, NT, 16]),
                            ALU.mult)
            ve.inc(v.tensor_tensor(
                lg3["t3"], lg3["fb"],
                R1["iuse"][:].unsqueeze(2).broadcast_to([128, NT, 16]), ALU.mult))
            v.wait_ge(ve.h, ve.n)
            ve.inc(v.tensor_tensor(lg2["maskf"], lg2["t1"], lg2["t3"], ALU.add))
            ve.inc(v.tensor_scalar(lg2["gated"], lg2["logit"], 0.0, None, ALU.max))
            v.wait_ge(ve.h, ve.n)
            ve.inc(v.reduce_max(R1["nmx"][:].unsqueeze(2), lg3["gated"], axis=AX.X,
                                negate=True))
            v.wait_ge(ve.h, ve.n)
            ve.inc(v.tensor_tensor(
                lg3["gated"], lg3["gated"],
                R1["nmx"][:].unsqueeze(2).broadcast_to([128, NT, 16]), ALU.add))

        @blk.scalar
        def _(s):
            s.wait_ge(ve.h, ve.n)
            ae.inc(s.activation(lg2["ex"], lg2["gated"], AF.Exp))

        @blk.vector
        def _(v):
            v.wait_ge(ae.h, ae.n)
            ve.inc(v.tensor_tensor(lg2["em"], lg2["ex"], lg2["maskf"], ALU.mult))
            v.wait_ge(ve.h, ve.n)
            ve.inc(v.reduce_sum(R1["sum"][:].unsqueeze(2), lg3["em"], axis=AX.X))
            v.wait_ge(ve.h, ve.n)
            ve.inc(v.reciprocal(R1["rin"][:], R1["sum"][:]))
            v.wait_ge(ve.h, ve.n)
            ve.inc(v.tensor_tensor(
                rw_all[:], lg3["em"],
                R1["rin"][:].unsqueeze(2).broadcast_to([128, NT, 16]), ALU.mult))

    b_pe, b_ve, b_so = pe.n, ve.n, so.n
    with nc.Block() as blk:
        @blk.tensor
        def _(t):
            for i in range(NT):
                t0 = 128 * i
                if i > 0:
                    t.wait_ge(ve.h, b_ve + i)
                for dc in range(8):
                    for fg in range(3):
                        mm = t.matmul(pA[:, 512 * fg:512 * (fg + 1)],
                                 xn_fm[:, dc, t0:t0 + 128],
                                 wqkv[:, dc, :, :].rearrange("p e k -> p (e k)")[
                                     :, 512 * fg:512 * (fg + 1)],
                                 start=(dc == 0), stop=(dc == 7))
                pe.inc(mm)
                for dc in range(8):
                    for fg in range(3):
                        mm = t.matmul(pB[:, 512 * fg:512 * (fg + 1)],
                                 xn_fm[:, dc, t0:t0 + 128],
                                 wqkv[:, dc, :, :].rearrange("p e k -> p (e k)")[
                                     :, 1536 + 512 * fg:1536 + 512 * (fg + 1)],
                                 start=(dc == 0), stop=(dc == 7))
                pe.inc(mm)

        @blk.vector
        def _(v):
            for i in range(NT):
                v.wait_ge(pe.h, b_pe + 2 * i + 2)
                if i >= 2:
                    v.wait_ge(so.h, b_so + 64 * (i - 2) + 64)
                v.tensor_scalar(acc[0][:], pA[:, 0:192], rw_all[:, i, 0:1], None,
                                ALU.mult)
                src = 0
                for e in range(1, 16):
                    psrc = pA if e < 8 else pB
                    off = 192 * (e % 8)
                    v.tensor_scalar(qst[i % 2][:], psrc[:, off:off + 192],
                                    rw_all[:, i, e:e + 1], None, ALU.mult)
                    v.tensor_tensor(acc[1 - src][:], qst[i % 2][:], acc[src][:],
                                    ALU.add)
                    src = 1 - src
                v.tensor_copy(qst[i % 2][:], acc[src][:]).then_inc(ve.h, 1)
                ve.n += 1

        @blk.sync
        def _(s):
            for i in range(NT):
                s.wait_ge(ve.h, b_ve + i + 1)
                so.dma(s.dma_start(o_q[128 * i:128 * (i + 1), :], qst[i % 2][:, 0:64]))
                so.dma(s.dma_start(o_k[128 * i:128 * (i + 1), :], qst[i % 2][:, 64:128]))
                so.dma(s.dma_start(o_v[128 * i:128 * (i + 1), :], qst[i % 2][:, 128:192]))
                so.dma(s.dma_start(o_rw[128 * i:128 * (i + 1), :], rw_all[:, i, :]))
    P.barrier()

    with nc.Block() as blk:
        @blk.sync
        def _(s):
            s.wait_ge(so.h, so.n)
    return P

def make_l1_inputs(x_slice, g):
    C = host_constants()
    m = {"x": x_slice,
         "wi": g["L0_attn_in_proj_w"],
         "wq": g["L0_ffn_q_proj"], "wk": g["L0_ffn_k_proj"], "wv": g["L0_ffn_v_proj"],
         "sim": g["L0_ffn_sim_matrix"],
         "gates": np.reshape(g["L0_ffn_gates"], (1, 16)),
         "alog": np.reshape(g["L0_attn_A_log"], (1, H)),
         "dtb": np.reshape(g["L0_attn_dt_bias"], (1, H)),
         "dpf": np.repeat(np.reshape(g["L0_attn_Dparam"], (H, 1)), PH, 1).reshape(1, DI),
         "niw": np.reshape(g["L0_attn_ni_w"], (1, D)),
         "nib": np.reshape(g["L0_attn_ni_b"], (1, D))}
    for n in CONST_L1:
        m["c_" + n] = C[n]
    return {k: np.ascontiguousarray(np.asarray(v, np.float32)) for k, v in m.items()}


KT = 64
NTL = 16
L2_STAGE = 99


def host_constants_l2():
    c = {}
    c["eye128"] = np.eye(128, dtype=np.float32)
    tri = np.where(np.arange(128)[:, None] > np.arange(128)[None, :], NEG, 0.0)
    c["triT"] = tri.astype(np.float32)
    c["onescol128"] = np.ones((128, 64), np.float32)
    orow = np.zeros((64, 128), np.float32); orow[0, :] = 1.0
    c["onesrow64"] = orow
    return c


CONST_L2 = ("eye128", "triT", "onescol128", "onesrow64")


def build_L2(stop_after=99):
    P = Prog("mk_l2")
    nc = P.nc
    C = host_constants_l2()

    yp_in = nc.dram_tensor("yp", [TOK, DI], F32, kind="ExternalInput").ap()
    z_in = nc.dram_tensor("z", [TOK, DI], F32, kind="ExternalInput").ap()
    cs_in = nc.dram_tensor("cs", [TOK, H], F32, kind="ExternalInput").ap()
    cfm_in = nc.dram_tensor("cfm", [64, TOK], F32, kind="ExternalInput").ap()
    q_in = nc.dram_tensor("q", [64, TOK], F32, kind="ExternalInput").ap()
    rw_in = nc.dram_tensor("rw", [NTL, 16, 128], F32, kind="ExternalInput").ap()
    x_in = nc.dram_tensor("x", [TOK, D], F32, kind="ExternalInput").ap()
    k_in = nc.dram_tensor("k", [64, T], F32, kind="ExternalInput").ap()
    v_in = nc.dram_tensor("v", [T, 64], F32, kind="ExternalInput").ap()
    hend_in = nc.dram_tensor("hend", [512, DI], F32, kind="ExternalInput").ap()
    sdec_in = nc.dram_tensor("sdec", [1, 256], F32, kind="ExternalInput").ap()
    wo_in = nc.dram_tensor("wo", [D, DI], F32, kind="ExternalInput").ap()
    op_in = nc.dram_tensor("op", [1024, D], F32, kind="ExternalInput").ap()
    lnw_in = nc.dram_tensor("lnw", [1, DI], F32, kind="ExternalInput").ap()
    lnb_in = nc.dram_tensor("lnb", [1, DI], F32, kind="ExternalInput").ap()
    now_in = nc.dram_tensor("now", [1, D], F32, kind="ExternalInput").ap()
    nob_in = nc.dram_tensor("nob", [1, D], F32, kind="ExternalInput").ap()
    msk_in = nc.dram_tensor("msk", [4, 128, 128], F32, kind="ExternalInput").ap()
    cin = {n: nc.dram_tensor("c_" + n, list(C[n].shape), F32,
                             kind="ExternalInput").ap() for n in CONST_L2}
    o_out = nc.dram_tensor("o_out", [TOK, D], F32, kind="ExternalOutput").ap()

    wob_dr = nc.dram_tensor("wob_dr", [D, DI], BF16, kind="Internal").ap()
    rwb_dr = nc.dram_tensor("rwb_dr", [NTL, 16 * 128], BF16, kind="Internal").ap()
    yn_dr = nc.dram_tensor("yn_dr", [TOK, DI], BF16, kind="Internal").ap()
    mo_dr = nc.dram_tensor("mo_dr", [TOK, D], BF16, kind="Internal").ap()
    row_dr = nc.dram_tensor("row_dr", [1, 512], F32, kind="Internal").ap()
    bias_dr = nc.dram_tensor("bias_dr", [1, D], F32, kind="Internal").ap()
    mx_dr = nc.dram_tensor("mx_dr", [1, TOK], F32, kind="Internal").ap()
    rs_dr = nc.dram_tensor("rs_dr", [1, TOK], F32, kind="Internal").ap()

    pA = nc.alloc_psum_tensor("pA", [128, 2048], F32)
    pB = nc.alloc_psum_tensor("pB", [128, 2048], F32)

    stk0 = contextlib.ExitStack()
    def sb(name, shape, dtype):
        return stk0.enter_context(nc.sbuf_tensor(name, shape, dtype))
    K = {n: sb("k_" + n, list(C[n].shape), BF16) for n in ("eye128", "triT")}
    edec_row = sb("edec_row", [1, 256], F32)
    sdec_row = sb("sdec_row", [1, 256], F32)
    edec_bc = sb("edec_bc", [64, 256], F32)
    rinv_sb = sb("rinv_sb", [128, 16], F32)
    nmx_sb = sb("nmx_sb", [128, 16], F32)
    eps_col = sb("eps_col", [128, 1], F32)
    stkWo = contextlib.ExitStack()
    WoT = stkWo.enter_context(nc.sbuf_tensor("WoT", [128, 16, 1024], BF16))

    ld = P.S("ld"); gld = P.S("gld"); sld = P.S("sld")
    so = P.S("so"); gso = P.S("gso")
    pe = P.S("pe"); ae = P.S("ae"); ve = P.S("ve"); po = P.S("po")

    def drain_and_return():
        with nc.Block() as blk:
            @blk.sync
            def _(s):
                for ds in (ld, gld, sld, so, gso):
                    s.wait_ge(ds.h, ds.n)
        return P

    with nc.Block() as blk:
        @blk.gpsimd
        def _(g):
            for n in ("eye128", "triT"):
                gld.dma(g.dma_start(K[n][:], cin[n]))
            gld.dma(g.dma_start(sdec_row[:], sdec_in))
            gld.dma(g.dma_start(rwb_dr, rw_in.rearrange("a e t -> a (e t)")))
            g.wait_ge(gld.h, gld.n)
            po.inc(g.nop())

        @blk.scalar
        def _(s):
            s.wait_ge(po.h, po.n)
            ae.inc(s.activation(edec_row[:], sdec_row[:], AF.Exp))

        @blk.gpsimd
        def _(g):
            g.wait_ge(ae.h, ae.n)
            gld.dma(g.dma_start(row_dr[0:1, 0:256], edec_row[:]))
            g.wait_ge(gld.h, gld.n)
            gld.dma(g.dma_start(edec_bc[:], row_dr[0:1, 0:256].broadcast_to([64, 256])))
            g.wait_ge(gld.h, gld.n)
            po.inc(g.nop())

        @blk.vector
        def _(v):
            pass
    P.barrier()

    stkW = contextlib.ExitStack()
    def sbW(name, shape, dtype):
        return stkW.enter_context(nc.sbuf_tensor(name, shape, dtype))
    lnw_bc = sbW("lnw_bc", [128, DI], F32)
    lnb_bc = sbW("lnb_bc", [128, DI], F32)
    wo_t = [sbW(f"wo_t{i}", [128, DI], F32) for i in range(2)]
    wos_t = [sbW(f"wos_t{i}", [128, DI], BF16) for i in range(2)]
    wb_t = sbW("wb_t", [128, DI], F32)
    bcol = sbW("bcol", [128, 1], F32)
    b_ld, b_ve, b_po, b_gso, b_so = ld.n, ve.n, po.n, gso.n, so.n
    with nc.Block() as blk:
        @blk.gpsimd
        def _(g):
            gld.dma(g.dma_start(lnw_bc[:], lnw_in.broadcast_to([128, DI])))
            gld.dma(g.dma_start(lnb_bc[:], lnb_in.broadcast_to([128, DI])))
            g.wait_ge(gld.h, gld.n)
            po.inc(g.nop())
    P.barrier()
    with nc.Block() as blk:
        @blk.sync
        def _(s):
            for dt_ in range(8):
                if dt_ >= 2:
                    s.wait_ge(ve.h, b_ve + 3 * (dt_ - 1))
                ld.dma(s.dma_start(wo_t[dt_ % 2][:],
                                   wo_in[128 * dt_:128 * (dt_ + 1), :]))
                s.wait_ge(ve.h, b_ve + 3 * dt_ + 3)
                so.dma(s.dma_start(bias_dr[0:1, 128 * dt_:128 * (dt_ + 1)]
                                   .rearrange("o p -> p o"), bcol[:]))

        @blk.vector
        def _(v):
            for dt_ in range(8):
                v.wait_ge(ld.h, b_ld + 16 * (dt_ + 1))
                if dt_ >= 2:
                    v.wait_ge(gso.h, b_gso + 16 * (dt_ - 1))
                v.tensor_tensor(wos_t[dt_ % 2][:], wo_t[dt_ % 2][:], lnw_bc[:],
                                ALU.mult).then_inc(ve.h, 1)
                ve.n += 1
                v.tensor_tensor(wb_t[:], wo_t[dt_ % 2][:], lnb_bc[:],
                                ALU.mult).then_inc(ve.h, 1)
                ve.n += 1
                v.wait_ge(ve.h, ve.n)
                if dt_ > 0:
                    v.wait_ge(so.h, b_so + 16 * dt_)
                v.reduce_sum(bcol[:], wb_t[:], axis=AX.X).then_inc(ve.h, 1)
                ve.n += 1

        @blk.gpsimd
        def _(g):
            for dt_ in range(8):
                g.wait_ge(ve.h, b_ve + 3 * dt_ + 1)
                gso.dma(g.dma_start(wob_dr[128 * dt_:128 * (dt_ + 1), :],
                                    wos_t[dt_ % 2][:]))
    P.barrier()
    stkW.close()

    with nc.Block() as blk:
        @blk.sync
        def _(s):
            for jc in range(16):
                sld.dma(s.dma_start(WoT[:, jc, :], wob_dr[:, 128 * jc:128 * (jc + 1)],
                                    transpose=True))

    P.barrier()

    stkH = contextlib.ExitStack()
    def sbH(name, shape, dtype):
        return stkH.enter_context(nc.sbuf_tensor(name, shape, dtype))
    Hst = sbH("Hst", [64, 8, DI], BF16)
    H_acc = sbH("H_acc", [64, DI], F32)
    he_t = [sbH(f"he_t{i}", [64, DI], F32) for i in range(2)]
    cfm_sb = sbH("cfm_sb", [64, NTL, 128], BF16)
    b_ld = ld.n
    b_ve0 = ve.n
    with nc.Block() as blk:
        @blk.gpsimd
        def _(g):
            gld.dma(g.dma_start(cfm_sb[:],
                                cfm_in.rearrange("n (a t) -> n a t", t=128)))
            po.inc(g.memset(Hst[:, 0, :], 0.0))

        @blk.sync
        def _(s):
            for sp in range(7):
                if sp >= 2:
                    s.wait_ge(ve.h, b_ve0 + 2 * (sp - 2) + 1)
                ld.dma(s.dma_start(he_t[sp % 2][:], hend_in[64 * sp:64 * (sp + 1), :]))

        @blk.vector
        def _(v):
            for sp in range(7):
                v.wait_ge(ld.h, b_ld + 16 * (sp + 1))
                if sp == 0:
                    ve.inc(v.tensor_copy(H_acc[:], he_t[0][:]))
                else:
                    H3 = H_acc[:].rearrange("n (h p) -> n h p", p=PH)
                    v.tensor_tensor(
                        H3, H3,
                        edec_bc[:, 32 * sp:32 * (sp + 1)].unsqueeze(2)
                        .broadcast_to([64, 32, PH]), ALU.mult)
                    ve.inc(v.tensor_tensor(H_acc[:], H_acc[:], he_t[sp % 2][:],
                                           ALU.add))
                v.wait_ge(ve.h, ve.n)
                ve.inc(v.tensor_copy(Hst[:, sp + 1, :], H_acc[:]))
    P.barrier()
    if stop_after <= 0:
        return drain_and_return()

    stkY = contextlib.ExitStack()
    def sbY(name, shape, dtype):
        return stkY.enter_context(nc.sbuf_tensor(name, shape, dtype))
    yp_t = [sbY(f"yp_t{i}", [128, DI], F32) for i in range(2)]
    z_t = [sbY(f"z_t{i}", [128, DI], F32) for i in range(2)]
    cs_t = [sbY(f"cs_t{i}", [128, H], F32) for i in range(2)]
    dcs_t = sbY("dcs_t", [128, H], F32)
    yent_s = sbY("yent_s", [128, DI], F32)
    sil_t = sbY("sil_t", [128, DI], F32)
    yg_t = sbY("yg_t", [128, DI], F32)
    sq_t = sbY("sq_t", [128, DI], F32)
    yn_t = [sbY(f"yn_t{i}", [128, DI], BF16) for i in range(2)]
    LNS = {n: sbY("ln_" + n, [128, 1], F32) for n in
           ("s1", "s2", "mu", "var", "tmp2", "tmpv", "srt", "rsg", "bxn")}
    with nc.Block() as blk:
        @blk.gpsimd
        def _(g):
            po.inc(g.memset(eps_col[:], EPS))
    P.barrier()

    b_ld, b_pe, b_ae, b_ve, b_po, b_so = ld.n, pe.n, ae.n, ve.n, po.n, so.n
    with nc.Block() as blk:
        @blk.sync
        def _(s):
            for j in range(NTL):
                if j >= 2:
                    s.wait_ge(po.h, b_po + 2 * (j - 1))
                    s.wait_ge(ae.h, b_ae + 5 * (j - 2) + 2)
                ld.dma(s.dma_start(yp_t[j % 2][:], yp_in[128 * j:128 * (j + 1), :]))
                ld.dma(s.dma_start(z_t[j % 2][:], z_in[128 * j:128 * (j + 1), :]))
                ld.dma(s.dma_start(cs_t[j % 2][:], cs_in[128 * j:128 * (j + 1), :]))
                if j >= 1:
                    s.wait_ge(ae.h, b_ae + 5 * (j - 1) + 5)
                    so.dma(s.dma_start(yn_dr[128 * (j - 1):128 * j, :],
                                       yn_t[(j - 1) % 2][:]))
            s.wait_ge(ae.h, b_ae + 5 * (NTL - 1) + 5)
            so.dma(s.dma_start(yn_dr[128 * (NTL - 1):128 * NTL, :],
                               yn_t[(NTL - 1) % 2][:]))

        @blk.tensor
        def _(t):
            for j in range(NTL):
                if j > 0:
                    t.wait_ge(ve.h, b_ve + 8 * (j - 1) + 1)
                for fg in range(4):
                    mm = t.matmul(pA[:, 512 * fg:512 * (fg + 1)], cfm_sb[:, j, :],
                             Hst[:, j // 2, 512 * fg:512 * (fg + 1)],
                             start=True, stop=True)
                pe.inc(mm)

        @blk.scalar
        def _(s):
            for j in range(NTL):
                s.wait_ge(ld.h, b_ld + 48 * (j + 1))
                if j > 0:
                    s.wait_ge(ve.h, b_ve + 8 * (j - 1) + 1)
                s.activation(dcs_t[:], cs_t[j % 2][:], AF.Exp).then_inc(ae.h, 1)
                ae.n += 1
                if j > 0:
                    s.wait_ge(po.h, b_po + 2 * j)
                s.activation(sil_t[:], z_t[j % 2][:], AF.Silu).then_inc(ae.h, 1)
                ae.n += 1
                s.wait_ge(po.h, b_po + 2 * j + 2)
                if j > 0:
                    s.wait_ge(ve.h, b_ve + 8 * (j - 1) + 3)
                s.activation(sq_t[:], yg_t[:], AF.Square).then_inc(ae.h, 1)
                ae.n += 1
                s.wait_ge(ve.h, b_ve + 8 * j + 6)
                s.activation(LNS["srt"][:], LNS["tmpv"][:], AF.Sqrt,
                             bias=eps_col[:]).then_inc(ae.h, 1)
                ae.n += 1
                s.wait_ge(ve.h, b_ve + 8 * j + 8)
                if j >= 2:
                    s.wait_ge(so.h, b_so + 16 * (j - 1))
                s.activation(yn_t[j % 2][:], yg_t[:], AF.Identity,
                             scale=LNS["rsg"][:], bias=LNS["bxn"][:]).then_inc(ae.h, 1)
                ae.n += 1

        @blk.gpsimd
        def _(g):
            for j in range(NTL):
                g.wait_ge(ld.h, b_ld + 48 * (j + 1))
                g.wait_ge(ve.h, b_ve + 8 * j + 1)
                if j > 0:
                    g.wait_ge(ae.h, b_ae + 5 * (j - 1) + 5)
                g.tensor_tensor(yg_t[:], yent_s[:], yp_t[j % 2][:],
                                ALU.add).then_inc(po.h, 1)
                po.n += 1
                g.wait_ge(ae.h, b_ae + 5 * j + 2)
                g.tensor_tensor(yg_t[:], yg_t[:], sil_t[:],
                                ALU.mult).then_inc(po.h, 1)
                po.n += 1

        @blk.vector
        def _(v):
            for j in range(NTL):
                v.wait_ge(pe.h, b_pe + j + 1)
                v.wait_ge(ae.h, b_ae + 5 * j + 1)
                v.tensor_tensor(yent_s[:].rearrange("p (h k) -> p h k", k=PH),
                                pA[:].rearrange("p (h k) -> p h k", k=PH),
                                dcs_t[:].unsqueeze(2).broadcast_to([128, H, PH]),
                                ALU.mult).then_inc(ve.h, 1)
                ve.n += 1
                v.wait_ge(po.h, b_po + 2 * j + 2)
                v.reduce_sum(LNS["s1"][:], yg_t[:], axis=AX.X,
                             negate=True).then_inc(ve.h, 1)
                ve.n += 1
                v.wait_ge(ae.h, b_ae + 5 * j + 3)
                v.reduce_sum(LNS["s2"][:], sq_t[:], axis=AX.X).then_inc(ve.h, 1)
                ve.n += 1
                v.wait_ge(ve.h, ve.n)
                v.tensor_scalar(LNS["mu"][:], LNS["s1"][:], 1.0 / DI, None, ALU.mult)
                ve.inc(v.tensor_scalar(LNS["tmp2"][:], LNS["s2"][:], 1.0 / DI,
                                       None, ALU.mult))
                v.wait_ge(ve.h, ve.n)
                ve.inc(v.scalar_tensor_tensor(LNS["var"][:], LNS["mu"][:], 1.0,
                                              LNS["mu"][:], ALU.mult, ALU.mult))
                v.wait_ge(ve.h, ve.n)
                ve.inc(v.scalar_tensor_tensor(LNS["tmpv"][:], LNS["var"][:], -1.0,
                                              LNS["tmp2"][:], ALU.mult, ALU.add))
                v.wait_ge(ae.h, b_ae + 5 * j + 4)
                ve.inc(v.reciprocal(LNS["rsg"][:], LNS["srt"][:]))
                v.wait_ge(ve.h, ve.n)
                ve.inc(v.tensor_tensor(LNS["bxn"][:], LNS["mu"][:], LNS["rsg"][:],
                                       ALU.mult))

    P.barrier()
    stkY.close()
    stkH.close()
    if stop_after <= 1:
        return drain_and_return()

    stkT = contextlib.ExitStack()
    def sbT(name, shape, dtype):
        return stkT.enter_context(nc.sbuf_tensor(name, shape, dtype))
    ynT = sbT("ynT", [128, 16, TOK], BF16)
    bias_bc0 = sbT("bias_bc0", [128, D], F32)
    with nc.Block() as blk:
        @blk.sync
        def _(s):
            for jc in range(16):
                sld.dma(s.dma_start(ynT[:, jc, :], yn_dr[:, 128 * jc:128 * (jc + 1)],
                                    transpose=True))

        @blk.gpsimd
        def _(g):
            gld.dma(g.dma_start(bias_bc0[:], bias_dr.broadcast_to([128, D])))
    P.barrier()
    if stop_after <= 2:
        return drain_and_return()

    mo1_t = sbT("mo1_t", [128, D], F32)
    sqm_t = sbT("sqm_t", [128, D], F32)
    bias_bc = bias_bc0
    mn_t = [sbT(f"mn_t{i}", [128, D], BF16) for i in range(2)]
    LNO = {n: sbT("lo_" + n, [128, 1], F32) for n in
           ("s1", "s2", "mu", "var", "tmp2", "tmpv", "srt", "rsg", "bxn")}
    b_pe, b_ae, b_ve, b_so = pe.n, ae.n, ve.n, so.n
    with nc.Block() as blk:
        @blk.tensor
        def _(t):
            for j in range(NTL):
                if j > 0:
                    t.wait_ge(ve.h, b_ve + 9 * (j - 1) + 1)
                for fg in range(2):
                    for ch in range(16):
                        mm = t.matmul(pA[:, 512 * fg:512 * (fg + 1)],
                                 ynT[:, ch, 128 * j:128 * (j + 1)],
                                 WoT[:, ch, 512 * fg:512 * (fg + 1)],
                                 start=(ch == 0), stop=(ch == 15))
                pe.inc(mm)

        @blk.vector
        def _(v):
            for j in range(NTL):
                v.wait_ge(pe.h, b_pe + j + 1)
                if j > 0:
                    v.wait_ge(ae.h, b_ae + 3 * (j - 1) + 3)
                v.scalar_tensor_tensor(mo1_t[:], pA[:, 0:1024], 1.0, bias_bc[:],
                                       ALU.mult, ALU.add).then_inc(ve.h, 1)
                ve.n += 1
                v.wait_ge(ve.h, ve.n)
                v.tensor_scalar(mo1_t[:], mo1_t[:], -10.0, 10.0,
                                ALU.max, ALU.min).then_inc(ve.h, 1)
                ve.n += 1
                v.wait_ge(ve.h, ve.n)
                v.reduce_sum(LNO["s1"][:], mo1_t[:], axis=AX.X,
                             negate=True).then_inc(ve.h, 1)
                ve.n += 1
                v.wait_ge(ae.h, b_ae + 3 * j + 1)
                v.reduce_sum(LNO["s2"][:], sqm_t[:], axis=AX.X).then_inc(ve.h, 1)
                ve.n += 1
                v.wait_ge(ve.h, ve.n)
                v.tensor_scalar(LNO["mu"][:], LNO["s1"][:], 1.0 / D, None, ALU.mult)
                ve.inc(v.tensor_scalar(LNO["tmp2"][:], LNO["s2"][:], 1.0 / D,
                                       None, ALU.mult))
                v.wait_ge(ve.h, ve.n)
                ve.inc(v.scalar_tensor_tensor(LNO["var"][:], LNO["mu"][:], 1.0,
                                              LNO["mu"][:], ALU.mult, ALU.mult))
                v.wait_ge(ve.h, ve.n)
                ve.inc(v.scalar_tensor_tensor(LNO["tmpv"][:], LNO["var"][:], -1.0,
                                              LNO["tmp2"][:], ALU.mult, ALU.add))
                v.wait_ge(ae.h, b_ae + 3 * j + 2)
                ve.inc(v.reciprocal(LNO["rsg"][:], LNO["srt"][:]))
                v.wait_ge(ve.h, ve.n)
                ve.inc(v.tensor_tensor(LNO["bxn"][:], LNO["mu"][:], LNO["rsg"][:],
                                       ALU.mult))

        @blk.scalar
        def _(s):
            for j in range(NTL):
                s.wait_ge(ve.h, b_ve + 9 * j + 2)
                if j > 0:
                    s.wait_ge(ve.h, b_ve + 9 * (j - 1) + 4)
                s.activation(sqm_t[:], mo1_t[:], AF.Square).then_inc(ae.h, 1)
                ae.n += 1
                s.wait_ge(ve.h, b_ve + 9 * j + 7)
                s.activation(LNO["srt"][:], LNO["tmpv"][:], AF.Sqrt,
                             bias=eps_col[:]).then_inc(ae.h, 1)
                ae.n += 1
                s.wait_ge(ve.h, b_ve + 9 * j + 9)
                if j >= 2:
                    s.wait_ge(so.h, b_so + 16 * (j - 1))
                s.activation(mn_t[j % 2][:], mo1_t[:], AF.Identity,
                             scale=LNO["rsg"][:], bias=LNO["bxn"][:]).then_inc(ae.h, 1)
                ae.n += 1

        @blk.sync
        def _(s2_):
            for j in range(NTL):
                s2_.wait_ge(ae.h, b_ae + 3 * j + 3)
                so.dma(s2_.dma_start(mo_dr[128 * j:128 * (j + 1), :], mn_t[j % 2][:]))
    P.barrier()
    stkT.close()
    stkWo.close()
    if stop_after <= 3:
        return drain_and_return()

    stkA = contextlib.ExitStack()
    def sbA(name, shape, dtype):
        return stkA.enter_context(nc.sbuf_tensor(name, shape, dtype))
    kfm_aug = sbA("kfm_aug", [65, T], BF16)
    qaug = sbA("qaug", [65, TOK], BF16)
    v_sb = sbA("v_sb", [128, KT, 65], BF16)
    OPT = sbA("OPT", [128, 8, 1024], BF16)
    msk_bf = sbA("msk_bf", [128, 4, 128], BF16)
    now_bc = sbA("now_bc", [128, D], F32)
    nob_bc = sbA("nob_bc", [128, D], F32)
    at_all = sbA("at_all", [128, NTL, 128], F32)
    mixT = sbA("mixT", [128, NTL, 8, 128], BF16)
    with nc.Block() as blk:
        @blk.gpsimd
        def _(g):
            gld.dma(g.dma_start(msk_bf[:], msk_in.rearrange("a u q -> u a q")))
            for ch in range(8):
                gld.dma(g.dma_start(OPT[:, ch, :], op_in[128 * ch:128 * (ch + 1), :]))
            gld.dma(g.dma_start(v_sb[:, :, 0:64],
                                v_in.rearrange("(a p) d -> p a d", p=128)))
            g.memset(v_sb[:, :, 64:65], 1.0)
            gld.dma(g.dma_start(now_bc[:], now_in.broadcast_to([128, D])))
            gld.dma(g.dma_start(nob_bc[:], nob_in.broadcast_to([128, D])))
            g.memset(kfm_aug[64:65, :], 1.0)
            g.memset(qaug[64:65, :], 0.0)

        @blk.gpsimd
        def _(g):
            gld.dma(g.dma_start(kfm_aug[0:64, :], k_in))
            gld.dma(g.dma_start(qaug[0:64, :], q_in))
            g.wait_ge(gld.h, gld.n)
            po.inc(g.nop())

        @blk.vector
        def _(v):
            v.wait_ge(po.h, po.n)
            ve.inc(v.tensor_scalar(qaug[0:64, :], qaug[0:64, :], 0.125, None,
                                   ALU.mult))
    P.barrier()

    mxcol = sbA("mxcol", [128, 16], F32)
    mx1 = sbA("mx1", [128, 1], F32)
    pt_sb = sbA("pt_sb", [128, 2, 512], BF16)
    b_pe, b_ve = pe.n, ve.n
    n_chunks = [j + 1 for j in range(NTL)]
    cum_ch = [sum(n_chunks[:j]) for j in range(NTL + 1)]
    chunk_inc = []
    for jj in range(NTL):
        for mm_ in range(jj + 1):
            chunk_inc.append(cum_ch[jj] + 2 * jj + mm_ + 1)
    with nc.Block() as blk:
        @blk.tensor
        def _(t):
            for j in range(NTL):
                for m in range(j + 1):
                    cidx = cum_ch[j] + m
                    if cidx >= 2:
                        t.wait_ge(ve.h, b_ve + chunk_inc[cidx - 2])
                    mm = t.matmul(pA[:, 512 * (cidx % 2):512 * (cidx % 2) + 512],
                             qaug[0:64, 128 * j:128 * (j + 1)],
                             kfm_aug[0:64, 512 * m:512 * (m + 1)],
                             start=True, stop=True)
                    pe.inc(mm)

        @blk.vector
        def _(v):
            for j in range(NTL):
                for m in range(j + 1):
                    cidx = cum_ch[j] + m
                    v.wait_ge(pe.h, b_pe + cidx + 1)
                    v.reduce_max(mxcol[:, m:m + 1],
                                 pA[:, 512 * (cidx % 2):512 * (cidx % 2) + 512],
                                 axis=AX.X).then_inc(ve.h, 1)
                    ve.n += 1
                v.wait_ge(ve.h, ve.n)
                v.reduce_max(mx1[:], mxcol[:, 0:j + 1], axis=AX.X).then_inc(ve.h, 1)
                ve.n += 1
                v.wait_ge(ve.h, ve.n)
                ve.inc(v.tensor_scalar(nmx_sb[:, j:j + 1], mx1[:], -1.0, None,
                                       ALU.mult))
    P.barrier()
    with nc.Block() as blk:
        @blk.sync
        def _(s):
            for j in range(NTL):
                so.dma(s.dma_start(
                    mx_dr[0:1, 128 * j:128 * (j + 1)].rearrange("o p -> p o"),
                    nmx_sb[:, j:j + 1]))

        @blk.gpsimd
        def _(g):
            g.wait_ge(so.h, so.n)
            gld.dma(g.dma_start(qaug[64:65, :], mx_dr))
            g.wait_ge(gld.h, gld.n)
            po.inc(g.nop())
    P.barrier()
    if stop_after <= 4:
        return drain_and_return()

    for s4 in range(4):
        jset = list(range(4 * s4, 4 * s4 + 4))
        ktmax = 16 * s4 + 15
        b_pe, b_ae, b_so, b_gso = pe.n, ae.n, so.n, gso.n
        with nc.Block() as blk:
            @blk.tensor
            def _(t, s4=s4, jset=jset, ktmax=ktmax, b_pe=b_pe, b_ae=b_ae):
                for kt in range(ktmax + 1):
                    elig = [j for j in jset if 4 * j + 3 >= kt]
                    sc = 512 * (kt % 2)
                    if kt >= 2:
                        t.wait_ge(ae.h, b_ae + kt - 1)
                    for idx, j in enumerate(elig):
                        last = t.matmul(pA[:, sc + 128 * idx:sc + 128 * (idx + 1)],
                                 kfm_aug[:, 128 * kt:128 * (kt + 1)],
                                 qaug[:, 128 * j:128 * (j + 1)],
                                 start=True, stop=(kt < 4 * j))
                        if kt >= 4 * j:
                            last = t.matmul(pA[:, sc + 128 * idx:sc + 128 * (idx + 1)],
                                     K["eye128"][:], msk_bf[:, kt - 4 * j, :],
                                     start=False, stop=True)
                    pe.inc(last)
                    t.wait_ge(ae.h, b_ae + kt + 1)
                    for idx, j in enumerate(elig):
                        jl = j - 4 * s4
                        mm = t.matmul(pB[0:65, 512 * jl:512 * jl + 128],
                                 v_sb[:, kt, :],
                                 pt_sb[:, kt % 2, 128 * idx:128 * (idx + 1)],
                                 start=(kt == 0), stop=(kt == 4 * j + 3))
                    pe.inc(mm)

            @blk.scalar
            def _(s, s4=s4, ktmax=ktmax, jset=jset, b_pe=b_pe, b_ae=b_ae):
                for kt in range(ktmax + 1):
                    elig = [j for j in jset if 4 * j + 3 >= kt]
                    nj = len(elig)
                    sc = 512 * (kt % 2)
                    s.wait_ge(pe.h, b_pe + 2 * kt + 1)
                    s.activation(pt_sb[:, kt % 2, 0:128 * nj],
                                 pA[:, sc:sc + 128 * nj], AF.Exp).then_inc(ae.h, 1)
                    ae.n += 1
        P.barrier()
        with nc.Block() as blk:
            @blk.scalar
            def _(s, s4=s4):
                ae.inc(s.activation(
                    at_all[0:65, 4 * s4:4 * s4 + 4, :],
                    pB[0:65, :].rearrange("p (a q) -> p a q", q=512)[:, :, 0:128],
                    AF.Copy))

            @blk.sync
            def _(s, s4=s4):
                s.wait_ge(ae.h, ae.n)
                for jl in range(4):
                    so.dma(s.dma_start(rs_dr[0:1, 128 * (4 * s4 + jl):
                                             128 * (4 * s4 + jl + 1)],
                                       at_all[64:65, 4 * s4 + jl, :]))

            @blk.gpsimd
            def _(g, s4=s4):
                g.wait_ge(ae.h, ae.n)
                g.wait_ge(so.h, so.n)
                gso.dma(g.dma_start(at_all[64:128, 4 * s4:4 * s4 + 4, :],
                                    at_all[0:64, 4 * s4:4 * s4 + 4, :]))
                g.wait_ge(gso.h, gso.n)
                po.inc(g.nop())
        P.barrier()
    if stop_after <= 5:
        return drain_and_return()

    rsraw = sbA("rsraw", [128, 16], F32)
    rw_bc = [sbA(f"rw_bc{i}", [128, TOK], BF16) for i in range(2)]
    mo_f = [sbA(f"mo_f{i}", [128, D], F32) for i in range(2)]
    x_t2 = [sbA(f"x_t2{i}", [128, D], F32) for i in range(2)]
    ffn_t = sbA("ffn_t", [128, D], F32)
    f1 = [sbA(f"f1{i}", [128, D], F32) for i in range(2)]
    with nc.Block() as blk:
        @blk.sync
        def _(s):
            for j in range(NTL):
                ld.dma(s.dma_start(
                    rsraw[:, j:j + 1],
                    rs_dr[0:1, 128 * j:128 * (j + 1)].rearrange("o p -> p o")))

        @blk.vector
        def _(v):
            v.wait_ge(ld.h, ld.n)
            ve.inc(v.reciprocal(rinv_sb[:], rsraw[:]))
    P.barrier()

    b_pe, b_ae, b_ve, b_po, b_ld, b_gld, b_so = \
        pe.n, ae.n, ve.n, po.n, ld.n, gld.n, so.n
    with nc.Block() as blk:
        @blk.gpsimd
        def _(g):
            for j in range(NTL):
                if j >= 2:
                    g.wait_ge(ve.h, b_ve + 3 * (j - 2) + 1)
                gld.dma(g.dma_start(
                    rw_bc[j % 2][:],
                    rwb_dr[j:j + 1, :].broadcast_to([128, 16 * 128])))
                if j >= 2:
                    g.wait_ge(ve.h, b_ve + 3 * (j - 2) + 2)
                gld.dma(g.dma_start(mo_f[j % 2][:], mo_dr[128 * j:128 * (j + 1), :]))
                g.wait_ge(ve.h, b_ve + 3 * j + 2)
                g.tensor_tensor(f1[j % 2][:], f1[j % 2][:], nob_bc[:],
                                ALU.add).then_inc(po.h, 1)
                po.n += 1
                g.wait_ge(ve.h, b_ve + 3 * j + 3)
                g.wait_ge(ae.h, b_ae + j + 1)
                g.tensor_tensor(f1[j % 2][:], f1[j % 2][:], ffn_t[:],
                                ALU.add).then_inc(po.h, 1)
                po.n += 1

        @blk.vector
        def _(v):
            for j in range(NTL):
                v.wait_ge(gld.h, b_gld + 32 * j + 16)
                for e in range(16):
                    ch, half = e // 2, e % 2
                    inst = v.tensor_tensor(
                        mixT[64 * half:64 * half + 64, j, ch, :],
                        at_all[64 * half:64 * half + 64, j, :],
                        rw_bc[j % 2][64 * half:64 * half + 64,
                                     128 * e:128 * (e + 1)],
                        ALU.mult)
                inst.then_inc(ve.h, 1)
                ve.n += 1
                v.wait_ge(gld.h, b_gld + 32 * (j + 1))
                if j >= 2:
                    v.wait_ge(so.h, b_so + 16 * (j - 1))
                v.tensor_tensor(f1[j % 2][:], mo_f[j % 2][:], now_bc[:],
                                ALU.mult).then_inc(ve.h, 1)
                ve.n += 1
                v.wait_ge(po.h, b_po + 2 * j + 1)
                v.wait_ge(ld.h, b_ld + 16 * (j + 1))
                v.tensor_tensor(f1[j % 2][:], f1[j % 2][:], x_t2[j % 2][:],
                                ALU.add).then_inc(ve.h, 1)
                ve.n += 1

        @blk.tensor
        def _(t):
            for j in range(NTL):
                t.wait_ge(ve.h, b_ve + 3 * j + 1)
                if j > 0:
                    t.wait_ge(ae.h, b_ae + j)
                for fg in range(2):
                    for ch in range(8):
                        mm = t.matmul(pA[:, 512 * fg:512 * (fg + 1)],
                                 mixT[:, j, ch, :],
                                 OPT[:, ch, 512 * fg:512 * (fg + 1)],
                                 start=(ch == 0), stop=(ch == 7))
                pe.inc(mm)

        @blk.scalar
        def _(s):
            for j in range(NTL):
                s.wait_ge(pe.h, b_pe + j + 1)
                if j > 0:
                    s.wait_ge(po.h, b_po + 2 * j)
                s.activation(ffn_t[:], pA[:, 0:1024], AF.Identity,
                             scale=rinv_sb[:, j:j + 1]).then_inc(ae.h, 1)
                ae.n += 1

        @blk.sync
        def _(s):
            for j in range(NTL):
                if j >= 2:
                    s.wait_ge(ve.h, b_ve + 3 * (j - 1) + 3)
                ld.dma(s.dma_start(x_t2[j % 2][:], x_in[128 * j:128 * (j + 1), :]))
                s.wait_ge(po.h, b_po + 2 * j + 2)
                so.dma(s.dma_start(o_out[128 * j:128 * (j + 1), :], f1[j % 2][:]))
    P.barrier()
    return drain_and_return()


def make_l2_inputs(c, l1, g):
    b, r = divmod(c, 4)
    C = host_constants_l2()
    cores = [4 * b + rr for rr in range(4)]
    def cat(name, axis=0):
        return np.concatenate([l1[f'c{cc}_{name}'] for cc in cores], axis)
    yp_all = cat('o_yp'); z_all = cat('o_z'); cs_all = cat('o_cs')
    q_all = cat('o_q'); rw_all_ = cat('o_rw'); cfm_all = cat('o_cfm', 1)
    k_all = cat('o_k'); v_all = cat('o_v')
    hend_all = cat('o_hend'); sdec_all = cat('o_sdec', 1)
    gsel = [4 * j + r for j in range(NTL)]
    rows = np.concatenate([np.arange(128 * gg, 128 * (gg + 1)) for gg in gsel])
    x_b = np.asarray(g['L0_attn_x'][b])
    msk = np.zeros((4, 128, 128), np.float32)
    u = np.arange(128)[:, None]; qq = np.arange(128)[None, :]
    for dkt in range(4):
        msk[dkt] = np.where(128 * dkt + u > 128 * r + qq, NEG, 0.0)
    m = {
        'yp': yp_all[rows], 'z': z_all[rows], 'cs': cs_all[rows],
        'cfm': np.concatenate([cfm_all[:, 128 * gg:128 * (gg + 1)] for gg in gsel], 1),
        'q': np.ascontiguousarray(q_all[rows].T),
        'rw': np.ascontiguousarray(
            rw_all_[rows].reshape(NTL, 128, 16).transpose(0, 2, 1)),
        'x': x_b[rows],
        'k': np.ascontiguousarray(k_all.T), 'v': v_all,
        'hend': hend_all, 'sdec': sdec_all,
        'wo': g['L0_attn_out_proj_w'],
        'op': g['L0_ffn_o_proj'].reshape(1024, 1024),
        'lnw': np.reshape(g['L0_attn_ln_w'], (1, DI)),
        'lnb': np.reshape(g['L0_attn_ln_b'], (1, DI)),
        'now': np.reshape(g['L0_attn_no_w'], (1, D)),
        'nob': np.reshape(g['L0_attn_no_b'], (1, D)),
        'msk': msk,
    }
    for n in CONST_L2:
        m['c_' + n] = C[n]
    return {kk: np.ascontiguousarray(np.asarray(vv, np.float32))
            for kk, vv in m.items()}


def kernel(**inputs):
    g = {k: np.asarray(v) for k, v in inputs.items()}
    x_full = g['L0_attn_x']
    P1 = build_L1()
    in1 = [make_l1_inputs(x_full[c // 4, 2048 * (c % 4):2048 * (c % 4) + 2048], g)
           for c in range(8)]
    res1 = run_bass_kernel_spmd(P1.nc, in1, core_ids=list(range(8)))
    l1 = {}
    for c in range(8):
        for kk, vv in res1.results[c].items():
            l1[f'c{c}_{kk}'] = np.asarray(vv)
    P2 = build_L2()
    in2 = [make_l2_inputs(c, l1, g) for c in range(8)]
    res2 = run_bass_kernel_spmd(P2.nc, in2, core_ids=list(range(8)))
    out = np.zeros((BATCH, T, D), np.float32)
    for c in range(8):
        b, r = divmod(c, 4)
        oo = np.asarray(res2.results[c]['o_out'])
        for j in range(NTL):
            gg = 4 * j + r
            out[b, 128 * gg:128 * (gg + 1)] = oo[128 * j:128 * (j + 1)]
    return out
